# version 21
# baseline (speedup 1.0000x reference)
import numpy as np
import concourse.bass as bass
import concourse.mybir as mybir
import concourse.bass_utils as _bass_utils
from concourse.bass_utils import run_bass_kernel_spmd
from contextlib import ExitStack

if not getattr(_bass_utils, "_ldw_opt_patched", False):
    _orig_run_command = _bass_utils.run_command

    def _run_command_ldw(cmd, *a, **kw):
        if isinstance(cmd, list):
            cmd = ["--enable-ldw-opt=true" if c == "--enable-ldw-opt=false" else c
                   for c in cmd]
        return _orig_run_command(cmd, *a, **kw)

    _bass_utils.run_command = _run_command_ldw
    _bass_utils._ldw_opt_patched = True

F32 = mybir.dt.float32
BF16 = mybir.dt.bfloat16
AF = mybir.ActivationFunctionType
ALU = mybir.AluOpType
AX = mybir.AxisListType

P = 128
OS = 512
E = 8
R = 16
ER = E * R
AGW = ER + E


class Prog:

    def __init__(self, refs=None):
        self.blocks = []
        self.cnt = {}
        self.ev = {}
        self.cur = None
        self.refs = refs
        self.waited = set()

    def new_block(self):
        self.cur = {}
        self.blocks.append(self.cur)
        return self.cur

    def op(self, eng, builder, ev=None, dma=False, sem=None):
        key = sem if sem is not None else eng
        do_inc = dma or self.refs is None or (ev is not None and ev in self.refs)
        if do_inc:
            self.cnt[key] = self.cnt.get(key, 0) + (16 if dma else 1)
        if ev is not None:
            assert ev not in self.ev, ev
            self.ev[ev] = (key, self.cnt.get(key, 0))
        self.cur.setdefault(eng, []).append(("op", builder, dma, key if do_inc else None))
        return ev

    def wait(self, eng, ev):
        self.waited.add(ev)
        self.cur.setdefault(eng, []).append(("waitev", ev, None))


def build_program(D=4096, TOK=1024, _refs=None):
    NT = TOK // P
    NJ = D // OS
    NK = D // P
    NCJ = NK // 2
    NKG = NK // 4
    NS = NT * NJ

    nc = bass.Bass("TRN2", target_bir_lowering=False)

    xt_d = nc.dram_tensor("xt", [D, TOK], F32, kind="ExternalInput").ap()
    xn_d = nc.dram_tensor("xn", [TOK, D], F32, kind="ExternalInput").ap()
    w1_d = nc.dram_tensor("w1t", [D, D], F32, kind="ExternalInput").ap()
    b1_d = nc.dram_tensor("b1", [1, D], F32, kind="ExternalInput").ap()
    w2_d = nc.dram_tensor("w2t", [D, D], F32, kind="ExternalInput").ap()
    b2_d = nc.dram_tensor("b2", [1, D], F32, kind="ExternalInput").ap()
    ag_d = nc.dram_tensor("ag", [D, AGW], F32, kind="ExternalInput").ap()
    bfl_d = nc.dram_tensor("bfl", [ER, D], F32, kind="ExternalInput").ap()
    id_d = nc.dram_tensor("ident", [P, P], F32, kind="ExternalInput").ap()
    out_d = nc.dram_tensor("out", [TOK, D], F32, kind="ExternalOutput").ap()

    xt_r = xt_d.rearrange("(n p) m -> n p m", p=P)
    w1_r = w1_d.rearrange("(n p) m -> n p m", p=P)
    w2_r = w2_d.rearrange("(n p) m -> n p m", p=P)
    ag_r = ag_d.rearrange("(n p) m -> n p m", p=P)

    es = ExitStack()
    with es:
        def sb(name, shape, dt):
            return es.enter_context(nc.sbuf_tensor(name, shape, dt))

        xt = sb("xt_sb", [P, NK * TOK], BF16)
        sbf = sb("sbf", [P, NT * D], BF16)
        wstg = [sb(f"wstg{i}", [P, max(1024, (NK // 4) * AGW)], F32) for i in range(2)]
        wbf = [sb(f"wbf{i}", [P, 1024], BF16) for i in range(3)]
        xstg = [sb(f"xstg{i}", [P, OS], F32) for i in range(2)]
        x1tmp = [sb(f"x1tmp{i}", [P, OS], F32) for i in range(4)]
        sqscr = sb("sqscr", [P, OS], F32)
        agbf = sb("agbf", [P, NK * AGW], BF16)
        bflbf = sb("bflbf", [P, D], BF16)
        G8 = min(8, NK)
        x1s = sb("x1s", [P, max(3 * OS, 2 * G8 * P)], BF16)
        hwstg = sb("hwstg", [P, NT * P], BF16)
        hwT = sb("hwT", [P, TOK], BF16)
        idbf = sb("idbf", [P, P], BF16)
        bbf = sb("bbf", [1, D], BF16)
        ones = sb("ones", [1, P], BF16)
        ssum = sb("ssum", [P, NS], F32)
        ssq = sb("ssq", [P, NS], F32)
        mean = sb("mean", [P, NT], F32)
        msq = sb("msq", [P, NT], F32)
        m2t = sb("m2t", [P, NT], F32)
        var = sb("var", [P, NT], F32)
        veps = sb("veps", [P, NT], F32)
        rin = sb("rin", [P, NT], F32)
        inv0 = sb("inv0", [P, NT], F32)
        invh = sb("invh", [P, NT], F32)
        nt1 = sb("nt1", [P, NT], F32)
        inv = sb("inv", [P, NT], F32)
        negmean = sb("negmean", [P, NT], F32)
        negminv = sb("negminv", [P, NT], F32)
        m1a = sb("m1a", [P, NT], F32)
        m2a = sb("m2a", [P, NT], F32)
        negm1a = sb("negm1a", [P, NT], F32)
        lg = sb("lg", [P, NT * E], F32)
        mask64 = sb("mask64", [P, NT * E], F32)
        lm64 = sb("lm64", [P, NT * E], F32)
        ex64 = sb("ex64", [P, NT * E], F32)
        ge64 = sb("ge64", [P, NT * E], F32)
        wa64 = sb("wa64", [P, NT * E], F32)
        em21 = sb("em21", [P, NT], F32)
        den = sb("den", [P, NT], F32)
        dinv = sb("dinv", [P, NT], F32)

        sems = {}
        for e in (["t", "v", "s", "q0", "q1"] + [f"g{i}" for i in range(7)]
                  + [f"o{i}" for i in range(7)] + [f"p{i}" for i in range(8)]):
            sems[e] = es.enter_context(nc.semaphore(f"sem_{e}"))

        PR = Prog(refs=_refs)
        engines_attr = {"q": "sync", "g": "gpsimd", "t": "tensor", "v": "vector", "s": "scalar"}

        def emit_block(plan):
            with nc.Block() as blk:
                for eng_name, items in plan.items():
                    def body(engine, items=items):
                        for it in items:
                            if it[0] == "waitev":
                                tgt, n = PR.ev[it[1]]
                                engine.wait_ge(sems[tgt], n)
                            else:
                                inst = it[1](engine)
                                if it[3] is not None:
                                    inst.then_inc(sems[it[3]], 16 if it[2] else 1)
                    getattr(blk, engines_attr[eng_name])(body)

        PR.new_block()
        pld = 0

        def q_dma(dst, src, slot, ev=None, wait_ev=None):
            if wait_ev is not None:
                PR.wait("q", wait_ev)
            PR.op("q", lambda e, d=dst, s=src: e.dma_start(out=d, in_=s), ev=ev, dma=True,
                  sem=f"q{slot}")

        def v_cast(dst, src, ev=None, wait_ev=None):
            if wait_ev is not None:
                PR.wait("v", wait_ev)
            PR.op("v", lambda e, d=dst, s=src: e.tensor_copy(out=d, in_=s), ev=ev)

        pre6 = xstg + x1tmp
        nbc = (D + 1023) // 1024
        items = []
        for c in range((D + OS - 1) // OS):
            cw = min(OS, D - c * OS)
            items.append((b1_d[:, c * OS:c * OS + cw], 1, cw, bbf[0:1, c * OS:c * OS + cw]))
        items.append((id_d, P, P, idbf[:]))
        for k in range(NK):
            items.append((ag_r[k], P, AGW, agbf[:, k * AGW:(k + 1) * AGW]))
        for c in range((D + OS - 1) // OS):
            cw = min(OS, D - c * OS)
            items.append((bfl_d[:, c * OS:c * OS + cw], P, cw, bflbf[:, c * OS:c * OS + cw]))
        for pi, (src, pp, w, dst) in enumerate(items):
            i = pi % 6
            deng = "q" if i < 3 else "s"
            if pi >= 6:
                PR.wait(deng, f"pc_{pi - 6}")
            PR.op(deng, lambda e, sl=pre6[i], src=src, pp=pp, w=w: e.dma_start(
                out=sl[0:pp, 0:w], in_=src), ev=f"pl_{pi}", dma=True, sem=f"p{i}")
            PR.wait("v", f"pl_{pi}")
            PR.op("v", lambda e, sl=pre6[i], pp=pp, w=w, dst=dst: e.tensor_copy(
                out=dst, in_=sl[0:pp, 0:w]), ev=f"pc_{pi}")
        PR.op("v", lambda e: e.memset(ones[:], 1.0))

        emit_block(PR.blocks[-1])

        def mm_phase(ph, w_r):
            sfx = str(ph)

            xsl6 = xstg + x1tmp
            NH = max(1, TOK // OS)
            W = TOK // NH
            first_new = 2 if ph == 2 else 0
            inj = min(10, NJ * NCJ - 1)

            def emit_hwt(e=None):
                PR.wait("q", f"hw_{NT - 1}")
                PR.op("q", lambda e: e.dma_start_transpose(
                    out=hwT[:].rearrange("p (k m) -> p k m", m=P),
                    in_=hwstg[:].rearrange("p (k m) -> p k m", m=P)),
                    ev="trhw", dma=True, sem="p2")

            if ph == 2 and inj < first_new:
                emit_hwt()
            for idx in range(first_new, NJ * NCJ):
                j, c = divmod(idx, NCJ)
                if ph == 1 and j == 0:
                    for hh in range(2 * NH):
                        h = 2 * NH * c + hh
                        k = h // NH
                        kk = h % NH
                        i = h % 6
                        if h >= 6:
                            PR.wait("q", f"pcx_{h - 6}")
                        PR.op("q", lambda e, sl=xsl6[i], k=k, kk=kk: e.dma_start(
                            out=sl[:, 0:W], in_=xt_r[k][:, kk * W:(kk + 1) * W]),
                            ev=f"plx_{h}", dma=True, sem=f"p{i}")
                        PR.wait("v", f"plx_{h}")
                        PR.op("v", lambda e, sl=xsl6[i], k=k, kk=kk: e.tensor_copy(
                            out=xt[:, k * TOK + kk * W: k * TOK + (kk + 1) * W],
                            in_=sl[:, 0:W]), ev=f"pcx_{h}")
                if ph == 2 and idx == inj and inj >= first_new:
                    emit_hwt()
                if idx >= 2:
                    PR.wait("q", f"cw{sfx}_{idx - 2}")
                q_dma(wstg[idx % 2][:, 0:OS], w_r[2 * c][:, j * OS:(j + 1) * OS], idx % 2)
                q_dma(wstg[idx % 2][:, OS:2 * OS], w_r[2 * c + 1][:, j * OS:(j + 1) * OS],
                      idx % 2, ev=f"lw{sfx}_{idx}")

            def s_cast(ci):
                PR.wait("s", f"lw{sfx}_{ci}")
                if ci >= 3:
                    PR.wait("s", f"mmw{sfx}_{ci - 3}")
                PR.op("s", lambda e, b=ci % 3, w=ci % 2: e.copy(
                    out=wbf[b][:], in_=wstg[w][:, 0:1024]), ev=f"cw{sfx}_{ci}")

            def s_x1act(ia):
                j, t = divmod(ia, NT)
                if ia >= 4:
                    PR.wait("s", f"stt{sfx}_{ia - 4}")
                PR.op("s", lambda e, b=ia % 4, t=t, j=j: e.activation(
                    out=x1tmp[b][:], in_=sbf[:, t * D + j * OS: t * D + (j + 1) * OS],
                    func=AF.Identity, bias=negminv[:, t:t + 1], scale=inv[:, t:t + 1]),
                    ev=f"xb_{ia}")

            vsq_tail = []

            def s_sq(idx):
                j, t = divmod(idx, NT)
                col = t * NJ + j
                if j == NJ - 1 and t % 2 == 1:
                    vsq_tail.append(idx)
                    return
                PR.wait("s", f"stt{sfx}_{idx}")
                if idx >= 1 and not (j == NJ - 1 and t == 1):
                    prev = idx - 1 if not (j == NJ - 1 and t >= 2) else idx - 2
                    PR.wait("s", f"sq{sfx}_{prev}")
                PR.op("s", lambda e, t=t, j=j, col=col: e.activation(
                    out=sqscr[:], in_=sbf[:, t * D + j * OS: t * D + (j + 1) * OS],
                    func=AF.Square, accum_out=ssq[:, col:col + 1]), ev=f"sq{sfx}_{idx}")

            half = NT // 2
            if ph == 1:
                for j in range(NJ):
                    if j == 0:
                        for c in range(NCJ):
                            s_cast(c)
                    if j < NJ - 1:
                        for c in range(min(3, NCJ)):
                            s_cast((j + 1) * NCJ + c)
                    for t in range(NT):
                        s_sq(j * NT + t)
                    if j < NJ - 1:
                        for c in range(3, NCJ):
                            s_cast((j + 1) * NCJ + c)
            else:
                for j in range(NJ):
                    if j == 0:
                        for c in range(2, NCJ):
                            s_cast(c)
                        for t in range(half):
                            s_x1act(t)
                    if j < NJ - 1:
                        s_cast((j + 1) * NCJ)
                    for t in range(half, NT):
                        s_x1act(j * NT + t)
                    if j < NJ - 1:
                        s_cast((j + 1) * NCJ + 1)
                        if NCJ > 2:
                            s_cast((j + 1) * NCJ + 2)
                        for t in range(half):
                            s_x1act((j + 1) * NT + t)
                    for t in range(NT):
                        s_sq(j * NT + t)
                    if j < NJ - 1:
                        for c in range(3, NCJ):
                            s_cast((j + 1) * NCJ + c)

            for j in range(NJ):
                boff = j * OS
                for t in range(NT):
                    if j > 0:
                        PR.wait("t", f"stt{sfx}_{(j - 1) * NT + t}")
                    elif ph == 2:
                        PR.wait("t", f"hw_{t}")
                    PR.op("t", lambda e, t=t, boff=boff: e.matmul(
                        ps[t][:], ones[0:1, 0:P], bbf[0:1, boff:boff + OS],
                        start=True, stop=False))
                for c in range(NCJ):
                    ci = j * NCJ + c
                    PR.wait("t", f"cw{sfx}_{ci}")
                    for kk in range(2):
                        k = 2 * c + kk
                        if ph == 1 and j == 0:
                            PR.wait("t", f"pcx_{(k + 1) * max(1, TOK // OS) - 1}")
                        last_k = (c == NCJ - 1 and kk == 1)
                        for t in range(NT):
                            ev = f"mmw{sfx}_{ci}" if (kk == 1 and t == NT - 1) else None
                            if last_k and ph == 1:
                                evy = f"y{sfx}_{j * NT + t}"
                                PR.op("t", lambda e, k=k, t=t, b=ci % 3, kk=kk: e.matmul(
                                    ps[t][:], xt[:, k * TOK + t * P: k * TOK + (t + 1) * P],
                                    wbf[b][:, kk * OS:(kk + 1) * OS],
                                    start=False, stop=True), ev=evy)
                                if t == NT - 1:
                                    PR.ev[f"mmw{sfx}_{ci}"] = PR.ev[evy]
                            else:
                                PR.op("t", lambda e, k=k, t=t, b=ci % 3, kk=kk: e.matmul(
                                    ps[t][:], xt[:, k * TOK + t * P: k * TOK + (t + 1) * P],
                                    wbf[b][:, kk * OS:(kk + 1) * OS],
                                    start=False, stop=False), ev=ev)
                if ph == 2:
                    for t in range(NT):
                        if j == 0 and t == 0:
                            PR.wait("t", "trhw")
                        PR.op("t", lambda e, t=t, boff=boff: e.matmul(
                            ps[t][:], hwT[:, t * P:(t + 1) * P], bflbf[:, boff:boff + OS],
                            start=False, stop=True), ev=f"y{sfx}_{j * NT + t}")

            if ph == 1:
                xslots = xstg + x1tmp
                nsl = len(xslots)
                nh2 = max(1, TOK // OS)
                htot = 2 * nh2 * NCJ
                for idx in range(NS):
                    j, t = divmod(idx, NT)
                    if idx < nsl:
                        hl = ((htot - 1 - idx % nsl) // nsl) * nsl + idx % nsl
                        if hl >= 0:
                            PR.wait("g", f"pcx_{hl}")
                    else:
                        PR.wait("g", f"stt{sfx}_{idx - nsl}")
                    PR.op("g", lambda e, b=idx % nsl, t=t, j=j: e.dma_start(
                        out=xslots[b][:], in_=xn_d[t * P:(t + 1) * P, j * OS:(j + 1) * OS]),
                        ev=f"lx_{idx}", dma=True, sem=f"g{idx % nsl}")

            for idx in range(NS):
                j, t = divmod(idx, NT)
                col = t * NJ + j
                PR.wait("v", f"y{sfx}_{idx}")
                if ph == 1:
                    PR.wait("v", f"lx_{idx}")
                else:
                    PR.wait("v", f"xb_{idx}")
                pool = (xstg + x1tmp) if ph == 1 else x1tmp
                PR.op("v", lambda e, t=t, j=j, b=idx % len(pool), pool=pool,
                      col=col: e.scalar_tensor_tensor(
                          out=sbf[:, t * D + j * OS: t * D + (j + 1) * OS],
                          in0=ps[t][:], scalar=0.0,
                          in1=pool[b][:],
                          op0=ALU.add, op1=ALU.add,
                          accum_out=ssum[:, col:col + 1]), ev=f"stt{sfx}_{idx}")
            for n_, idx in enumerate(vsq_tail):
                j, t = divmod(idx, NT)
                col = t * NJ + j
                PR.wait("v", f"stt{sfx}_{idx}")
                if n_ >= 1:
                    PR.wait("v", f"sq{sfx}_{vsq_tail[n_ - 1]}")
                PR.op("v", lambda e, t=t, j=j, col=col: e.scalar_tensor_tensor(
                    out=xstg[0][:], in0=sbf[:, t * D + j * OS: t * D + (j + 1) * OS],
                    scalar=1.0, in1=sbf[:, t * D + j * OS: t * D + (j + 1) * OS],
                    op0=ALU.mult, op1=ALU.mult,
                    accum_out=ssq[:, col:col + 1]), ev=f"sq{sfx}_{idx}")

        def ln_stats(ph):
            sfx = str(ph)
            PR.op("v", lambda e: e.reduce_sum(
                mean[:], ssum[:].rearrange("p (t j) -> p t j", j=NJ), axis=AX.X),
                ev=f"r1_{sfx}")
            PR.op("v", lambda e: e.reduce_sum(
                msq[:], ssq[:].rearrange("p (t j) -> p t j", j=NJ), axis=AX.X),
                ev=f"r2_{sfx}")
            PR.wait("v", f"r1_{sfx}")
            PR.op("v", lambda e: e.tensor_scalar_mul(out=mean[:], in0=mean[:], scalar1=1.0 / D),
                  ev=f"mn_{sfx}")
            PR.wait("v", f"r2_{sfx}")
            PR.op("v", lambda e: e.tensor_scalar_mul(out=msq[:], in0=msq[:], scalar1=1.0 / D))
            PR.wait("v", f"mn_{sfx}")
            PR.op("v", lambda e: e.tensor_mul(m2t[:], mean[:], mean[:]), ev=f"m2_{sfx}")
            PR.wait("v", f"m2_{sfx}")
            PR.op("v", lambda e: e.tensor_sub(var[:], msq[:], m2t[:]), ev=f"va_{sfx}")
            PR.wait("v", f"va_{sfx}")
            PR.op("v", lambda e: e.tensor_scalar_add(out=veps[:], in0=var[:], scalar1=1e-5),
                  ev=f"ve_{sfx}")
            PR.wait("v", f"ve_{sfx}")
            PR.op("v", lambda e: e.reciprocal(out=rin[:], in_=veps[:]), ev=f"ri_{sfx}")
            PR.wait("s", f"ri_{sfx}")
            PR.op("s", lambda e: e.activation(out=inv0[:], in_=rin[:], func=AF.Sqrt),
                  ev=f"i0_{sfx}")
            PR.wait("s", f"mn_{sfx}")
            PR.op("s", lambda e: e.mul(out=negmean[:], in_=mean[:], mul=-1.0), ev=f"ng_{sfx}")
            PR.wait("v", f"i0_{sfx}")
            PR.op("v", lambda e: e.tensor_scalar(out=invh[:], in0=veps[:], scalar1=0.5,
                                                 scalar2=None, op0=ALU.mult), ev=f"ih_{sfx}")
            PR.op("v", lambda e: e.tensor_mul(nt1[:], inv0[:], inv0[:]), ev=f"n1_{sfx}")
            PR.wait("v", f"n1_{sfx}")
            PR.op("v", lambda e: e.tensor_mul(nt1[:], nt1[:], invh[:]), ev=f"n2_{sfx}")
            PR.wait("v", f"n2_{sfx}")
            PR.op("v", lambda e: e.tensor_scalar(out=nt1[:], in0=nt1[:], scalar1=-1.0,
                                                 scalar2=1.5, op0=ALU.mult, op1=ALU.add),
                  ev=f"n3_{sfx}")
            PR.wait("v", f"n3_{sfx}")
            PR.op("v", lambda e: e.tensor_mul(inv[:], inv0[:], nt1[:]), ev=f"iv_{sfx}")
            PR.wait("v", f"ng_{sfx}")
            PR.wait("v", f"iv_{sfx}")
            PR.op("v", lambda e: e.tensor_mul(negminv[:], negmean[:], inv[:]), ev=f"nmi_{sfx}")

        with ExitStack() as ps_es:
            ps = [ps_es.enter_context(nc.psum_tensor(f"ps{i}", [P, OS], F32))
                  for i in range(NT)]
            PR.new_block()
            mm_phase(1, w1_r)
            emit_block(PR.blocks[-1])

        xsl = [x1s[:, 0:OS], x1s[:, OS:2 * OS], x1s[:, 2 * OS:3 * OS]]
        with ExitStack() as ps_es:
            pst = [ps_es.enter_context(nc.psum_tensor(f"pst{i}", [P, 1024], BF16))
                   for i in range(3)]
            pwm = [ps_es.enter_context(nc.psum_tensor(f"pwm{i}", [P, OS], F32))
                   for i in range(2)]
            PR.new_block()
            for wi in range(24):
                if wi >= 2:
                    PR.wait("t", f"wrm_{wi - 2}")
                PR.op("t", lambda e, b=wi % 2: e.matmul(
                    pwm[b][:], xt[:, 0:P], bflbf[:, 0:OS], start=True, stop=True),
                    ev=f"wrm_{wi}")
            ln_stats(1)
            for idx in range(NT * NKG):
                t, kg = divmod(idx, NKG)
                if idx == 0:
                    PR.wait("s", "nmi_1")
                if idx >= 3:
                    PR.wait("s", f"tr1_{idx - 3}")
                PR.op("s", lambda e, b=idx % 3, t=t, kg=kg: e.activation(
                    out=xsl[b], in_=sbf[:, t * D + kg * OS: t * D + (kg + 1) * OS],
                    func=AF.Identity, bias=negminv[:, t:t + 1], scale=inv[:, t:t + 1]),
                    ev=f"xa_{idx}")
                PR.wait("t", f"xa_{idx}")
                if idx >= 3:
                    PR.wait("t", f"xe_{idx - 3}")
                for q in range(4):
                    PR.op("t", lambda e, b=idx % 3, q=q: e.transpose(
                        pst[b][:, q * P:(q + 1) * P], xsl[b][:, q * P:(q + 1) * P],
                        idbf[:]), ev=f"tr1_{idx}" if q == 3 else None)
                PR.wait("v", f"tr1_{idx}")
                PR.op("v", lambda e, b=idx % 3, t=t, kg=kg: e.tensor_copy(
                    out=xt[:].rearrange("p (k m) -> p k m", m=TOK)[
                        :, 4 * kg:4 * kg + 4, t * P:(t + 1) * P],
                    in_=pst[b][:, 0:4 * P].rearrange("p (k m) -> p k m", m=P)),
                    ev=f"xe_{idx}")
            emit_block(PR.blocks[-1])

        with ExitStack() as ps_es:
            ps = [ps_es.enter_context(nc.psum_tensor(f"psb{i}", [P, OS], F32))
                  for i in range(NT)]
            psh = ps
            PR.new_block()
            for c in range(nbc):
                cw = min(1024, D - c * 1024)
                q_dma(wstg[c % 2][0:1, 0:cw], b2_d[:, c * 1024:c * 1024 + cw], c % 2,
                      ev=f"lb2_{c}", wait_ev=(f"cb2_{c - 2}" if c >= 2 else None))
                v_cast(bbf[0:1, c * 1024:c * 1024 + cw], wstg[c % 2][0:1, 0:cw],
                       ev=f"cb2_{c}", wait_ev=f"lb2_{c}")
            for idx in range(2):
                j, c = divmod(idx, NCJ)
                if nbc > idx:
                    PR.wait("q", f"cb2_{max(0, nbc - 2 + idx)}")
                q_dma(wstg[idx % 2][:, 0:OS], w2_r[2 * c][:, j * OS:(j + 1) * OS], idx % 2)
                q_dma(wstg[idx % 2][:, OS:2 * OS], w2_r[2 * c + 1][:, j * OS:(j + 1) * OS],
                      idx % 2, ev=f"lw2_{idx}")
            for ci in range(2):
                PR.wait("s", f"lw2_{ci}")
                PR.op("s", lambda e, b=ci % 3, w=ci % 2: e.copy(
                    out=wbf[b][:], in_=wstg[w][:, 0:1024]), ev=f"cw2_{ci}")
            for t in range(NT):
                for k in range(NK):
                    PR.op("t", lambda e, t=t, k=k: e.matmul(
                        psh[t][:, 0:AGW], xt[:, k * TOK + t * P: k * TOK + (t + 1) * P],
                        agbf[:, k * AGW:(k + 1) * AGW],
                        start=(k == 0), stop=(k == NK - 1)),
                        ev=f"h_{t}" if k == NK - 1 else None)
            for t in range(NT):
                PR.wait("v", f"h_{t}")
                PR.op("v", lambda e, t=t: e.tensor_copy(
                    out=lg[:, t * E:(t + 1) * E], in_=psh[t][:, ER:ER + E]),
                    ev=f"lg_{t}" if t == NT - 1 else None)
            PR.wait("v", f"lg_{NT - 1}")
            PR.op("v", lambda e: e.reduce_max(
                m1a[:], lg[:].rearrange("p (t e) -> p t e", e=E), axis=AX.X),
                ev="m1all")
            PR.wait("s", "m1all")
            PR.op("s", lambda e: e.mul(out=negm1a[:], in_=m1a[:], mul=-1.0), ev="ngall")
            PR.wait("s", "ngall")
            for t in range(NT):
                PR.op("s", lambda e, t=t: e.activation(
                    out=ex64[:, t * E:(t + 1) * E], in_=lg[:, t * E:(t + 1) * E],
                    func=AF.Exp, bias=negm1a[:, t:t + 1]),
                    ev="exall" if t == NT - 1 else None)
            PR.wait("v", "m1all")
            for t in range(NT):
                PR.op("v", lambda e, t=t: e.tensor_single_scalar(
                    out=mask64[:, t * E:(t + 1) * E], in_=lg[:, t * E:(t + 1) * E],
                    scalar=m1a[:, t:t + 1], op=ALU.is_equal),
                    ev="mkall" if t == NT - 1 else None)
            PR.wait("v", "mkall")
            for t in range(NT):
                PR.op("v", lambda e, t=t: e.scalar_tensor_tensor(
                    out=lm64[:, t * E:(t + 1) * E], in0=mask64[:, t * E:(t + 1) * E],
                    scalar=-1e30, in1=lg[:, t * E:(t + 1) * E],
                    op0=ALU.mult, op1=ALU.add),
                    ev="lmall" if t == NT - 1 else None)
            PR.wait("v", "lmall")
            PR.op("v", lambda e: e.reduce_max(
                m2a[:], lm64[:].rearrange("p (t e) -> p t e", e=E), axis=AX.X),
                ev="m2all")
            PR.wait("s", "m2all")
            for t in range(NT):
                PR.op("s", lambda e, t=t: e.activation(
                    out=em21[:, t:t + 1], in_=m2a[:, t:t + 1], func=AF.Exp,
                    bias=negm1a[:, t:t + 1]), ev="e21all" if t == NT - 1 else None)
            PR.wait("v", "e21all")
            PR.op("v", lambda e: e.tensor_scalar_add(out=den[:], in0=em21[:], scalar1=1.0),
                  ev="dnall")
            PR.wait("v", "dnall")
            PR.op("v", lambda e: e.reciprocal(out=dinv[:], in_=den[:]), ev="diall")
            PR.wait("v", "m2all")
            for t in range(NT):
                PR.op("v", lambda e, t=t: e.tensor_single_scalar(
                    out=ge64[:, t * E:(t + 1) * E], in_=lg[:, t * E:(t + 1) * E],
                    scalar=m2a[:, t:t + 1], op=ALU.is_ge),
                    ev="geall" if t == NT - 1 else None)
            PR.wait("v", "geall")
            PR.wait("v", "diall")
            for t in range(NT):
                PR.op("v", lambda e, t=t: e.scalar_tensor_tensor(
                    out=wa64[:, t * E:(t + 1) * E], in0=ex64[:, t * E:(t + 1) * E],
                    scalar=dinv[:, t:t + 1], in1=ge64[:, t * E:(t + 1) * E],
                    op0=ALU.mult, op1=ALU.mult),
                    ev="waall" if t == NT - 1 else None)
            PR.wait("v", "waall")
            for t in range(NT):
                for ee in range(E):
                    PR.op("v", lambda e, t=t, ee=ee: e.tensor_single_scalar(
                        out=hwstg[:, t * P + ee * R: t * P + (ee + 1) * R],
                        in_=psh[t][:, ee * R:(ee + 1) * R], scalar=wa64[:, t * E + ee:t * E + ee + 1],
                        op=ALU.mult), ev=f"hw_{t}" if ee == E - 1 else None)
            mm_phase(2, w2_r)
            emit_block(PR.blocks[-1])

        PR.new_block()
        ln_stats(2)
        PR.wait("s", "nmi_2")
        PR.wait("v", "iv_2")
        oslots = x1tmp + xstg
        nos = len(oslots)
        for idx in range(NS):
            t, j = divmod(idx, NJ)
            b = idx % nos
            sl = sbf[:, t * D + j * OS: t * D + (j + 1) * OS]
            aeng = "s" if idx % 2 == 0 else "v"
            if idx >= nos:
                PR.wait(aeng, f"so_{idx - nos}")
            if aeng == "s":
                PR.op("s", lambda e, b=b, sl=sl, t=t: e.activation(
                    out=oslots[b][:], in_=sl, func=AF.Identity,
                    bias=negminv[:, t:t + 1], scale=inv[:, t:t + 1]), ev=f"oa_{idx}")
            else:
                PR.op("v", lambda e, b=b, sl=sl, t=t: e.tensor_scalar(
                    out=oslots[b][:], in0=sl, scalar1=mean[:, t:t + 1],
                    scalar2=inv[:, t:t + 1], op0=ALU.subtract, op1=ALU.mult),
                    ev=f"oa_{idx}")
            qeng = "q" if b < 5 else "g"
            PR.wait(qeng, f"oa_{idx}")
            PR.op(qeng, lambda e, b=b, t=t, j=j: e.dma_start(
                out=out_d[t * P:(t + 1) * P, j * OS:(j + 1) * OS], in_=oslots[b][:]),
                ev=f"so_{idx}", dma=True, sem=f"o{idx % nos}")
        for k in range(min(nos, NS)):
            PR.wait("g", f"so_{NS - 1 - k}")
        emit_block(PR.blocks[-1])

    if _refs is None:
        return build_program(D=D, TOK=TOK, _refs=PR.waited)
    return nc


def kernel(L0_attn_x, L0_attn_weight, L0_attn_bias, L0_ffn_W_base, L0_ffn_b_base,
           L0_ffn_W_gate, L0_ffn_A, L0_ffn_B):
    D = 4096
    TOK = 1024
    n_cores = 8
    x2d = np.ascontiguousarray(L0_attn_x.reshape(n_cores * TOK, D))
    w1t = np.ascontiguousarray(L0_attn_weight.T)
    b1 = np.ascontiguousarray(L0_attn_bias.reshape(1, D))
    w2t = np.ascontiguousarray(L0_ffn_W_base.T)
    b2 = np.ascontiguousarray(L0_ffn_b_base.reshape(1, D))
    ag = np.ascontiguousarray(
        np.concatenate([L0_ffn_A.reshape(ER, D), L0_ffn_W_gate], axis=0).T)
    bfl = np.ascontiguousarray(L0_ffn_B.transpose(0, 2, 1).reshape(ER, D))
    ident = np.eye(P, dtype=np.float32)

    nc = build_program(D=D, TOK=TOK)
    in_maps = []
    for c in range(n_cores):
        xn_c = np.ascontiguousarray(x2d[c * TOK:(c + 1) * TOK])
        xt_c = np.ascontiguousarray(xn_c.T)
        in_maps.append({"xt": xt_c, "xn": xn_c, "w1t": w1t, "b1": b1,
                        "w2t": w2t, "b2": b2, "ag": ag, "bfl": bfl, "ident": ident})
    res = run_bass_kernel_spmd(nc, in_maps, core_ids=list(range(n_cores)))
    out = np.concatenate([np.asarray(r["out"]) for r in res.results], axis=0)
    return out.reshape(L0_attn_x.shape)


# version 22
# speedup vs baseline: 1.0216x; 1.0216x over previous
import numpy as np
import concourse.bass as bass
import concourse.mybir as mybir
from concourse.bass_utils import run_bass_kernel_spmd
from contextlib import ExitStack

F32 = mybir.dt.float32
BF16 = mybir.dt.bfloat16
AF = mybir.ActivationFunctionType
ALU = mybir.AluOpType
AX = mybir.AxisListType

P = 128
OS = 512
E = 8
R = 16
ER = E * R
AGW = ER + E


class Prog:

    def __init__(self, refs=None):
        self.blocks = []
        self.cnt = {}
        self.ev = {}
        self.cur = None
        self.refs = refs
        self.waited = set()

    def new_block(self):
        self.cur = {}
        self.blocks.append(self.cur)
        return self.cur

    def op(self, eng, builder, ev=None, dma=False, sem=None):
        key = sem if sem is not None else eng
        do_inc = dma or self.refs is None or (ev is not None and ev in self.refs)
        if do_inc:
            self.cnt[key] = self.cnt.get(key, 0) + (16 if dma else 1)
        if ev is not None:
            assert ev not in self.ev, ev
            self.ev[ev] = (key, self.cnt.get(key, 0))
        self.cur.setdefault(eng, []).append(("op", builder, dma, key if do_inc else None))
        return ev

    def wait(self, eng, ev):
        self.waited.add(ev)
        self.cur.setdefault(eng, []).append(("waitev", ev, None))


def build_program(D=4096, TOK=1024, _refs=None):
    NT = TOK // P
    NJ = D // OS
    NK = D // P
    NCJ = NK // 2
    NKG = NK // 4
    NS = NT * NJ

    nc = bass.Bass("TRN2", target_bir_lowering=False)

    xt_d = nc.dram_tensor("xt", [D, TOK], F32, kind="ExternalInput").ap()
    xn_d = nc.dram_tensor("xn", [TOK, D], F32, kind="ExternalInput").ap()
    w1_d = nc.dram_tensor("w1t", [D, D], F32, kind="ExternalInput").ap()
    b1_d = nc.dram_tensor("b1", [1, D], F32, kind="ExternalInput").ap()
    w2_d = nc.dram_tensor("w2t", [D, D], F32, kind="ExternalInput").ap()
    b2_d = nc.dram_tensor("b2", [1, D], F32, kind="ExternalInput").ap()
    ag_d = nc.dram_tensor("ag", [D, AGW], F32, kind="ExternalInput").ap()
    bfl_d = nc.dram_tensor("bfl", [ER, D], F32, kind="ExternalInput").ap()
    id_d = nc.dram_tensor("ident", [P, P], F32, kind="ExternalInput").ap()
    out_d = nc.dram_tensor("out", [TOK, D], F32, kind="ExternalOutput").ap()

    xt_r = xt_d.rearrange("(n p) m -> n p m", p=P)
    w1_r = w1_d.rearrange("(n p) m -> n p m", p=P)
    w2_r = w2_d.rearrange("(n p) m -> n p m", p=P)
    ag_r = ag_d.rearrange("(n p) m -> n p m", p=P)

    es = ExitStack()
    with es:
        def sb(name, shape, dt):
            return es.enter_context(nc.sbuf_tensor(name, shape, dt))

        xt = sb("xt_sb", [P, NK * TOK], BF16)
        sbf = sb("sbf", [P, NT * D], BF16)
        wstg = [sb(f"wstg{i}", [P, max(1024, (NK // 4) * AGW)], F32) for i in range(2)]
        wbf = [sb(f"wbf{i}", [P, 1024], BF16) for i in range(3)]
        xstg = [sb(f"xstg{i}", [P, OS], F32) for i in range(2)]
        x1tmp = [sb(f"x1tmp{i}", [P, OS], F32) for i in range(4)]
        sqscr = sb("sqscr", [P, OS], F32)
        agbf = sb("agbf", [P, NK * AGW], BF16)
        bflbf = sb("bflbf", [P, D], BF16)
        G8 = min(8, NK)
        x1s = sb("x1s", [P, max(3 * OS, 2 * G8 * P)], BF16)
        hwstg = sb("hwstg", [P, NT * P], BF16)
        hwT = sb("hwT", [P, TOK], BF16)
        idbf = sb("idbf", [P, P], BF16)
        bbf = sb("bbf", [1, D], BF16)
        ones = sb("ones", [1, P], BF16)
        ssum = sb("ssum", [P, NS], F32)
        ssq = sb("ssq", [P, NS], F32)
        mean = sb("mean", [P, NT], F32)
        msq = sb("msq", [P, NT], F32)
        m2t = sb("m2t", [P, NT], F32)
        var = sb("var", [P, NT], F32)
        veps = sb("veps", [P, NT], F32)
        rin = sb("rin", [P, NT], F32)
        inv0 = sb("inv0", [P, NT], F32)
        invh = sb("invh", [P, NT], F32)
        nt1 = sb("nt1", [P, NT], F32)
        inv = sb("inv", [P, NT], F32)
        negmean = sb("negmean", [P, NT], F32)
        negminv = sb("negminv", [P, NT], F32)
        m1a = sb("m1a", [P, NT], F32)
        m2a = sb("m2a", [P, NT], F32)
        negm1a = sb("negm1a", [P, NT], F32)
        lg = sb("lg", [P, NT * E], F32)
        mask64 = sb("mask64", [P, NT * E], F32)
        lm64 = sb("lm64", [P, NT * E], F32)
        ex64 = sb("ex64", [P, NT * E], F32)
        ge64 = sb("ge64", [P, NT * E], F32)
        wa64 = sb("wa64", [P, NT * E], F32)
        em21 = sb("em21", [P, NT], F32)
        den = sb("den", [P, NT], F32)
        dinv = sb("dinv", [P, NT], F32)

        sems = {}
        for e in (["t", "v", "s", "q0", "q1"] + [f"g{i}" for i in range(7)]
                  + [f"o{i}" for i in range(7)] + [f"p{i}" for i in range(8)]):
            sems[e] = es.enter_context(nc.semaphore(f"sem_{e}"))

        PR = Prog(refs=_refs)
        engines_attr = {"q": "sync", "g": "gpsimd", "t": "tensor", "v": "vector", "s": "scalar"}

        def emit_block(plan):
            with nc.Block() as blk:
                for eng_name, items in plan.items():
                    def body(engine, items=items):
                        for it in items:
                            if it[0] == "waitev":
                                tgt, n = PR.ev[it[1]]
                                engine.wait_ge(sems[tgt], n)
                            else:
                                inst = it[1](engine)
                                if it[3] is not None:
                                    inst.then_inc(sems[it[3]], 16 if it[2] else 1)
                    getattr(blk, engines_attr[eng_name])(body)

        PR.new_block()
        pld = 0

        def q_dma(dst, src, slot, ev=None, wait_ev=None):
            if wait_ev is not None:
                PR.wait("q", wait_ev)
            PR.op("q", lambda e, d=dst, s=src: e.dma_start(out=d, in_=s), ev=ev, dma=True,
                  sem=f"q{slot}")

        def v_cast(dst, src, ev=None, wait_ev=None):
            if wait_ev is not None:
                PR.wait("v", wait_ev)
            PR.op("v", lambda e, d=dst, s=src: e.tensor_copy(out=d, in_=s), ev=ev)

        pre6 = xstg + x1tmp
        nbc = (D + 1023) // 1024
        items = []
        for c in range((D + OS - 1) // OS):
            cw = min(OS, D - c * OS)
            items.append((b1_d[:, c * OS:c * OS + cw], 1, cw, bbf[0:1, c * OS:c * OS + cw]))
        items.append((id_d, P, P, idbf[:]))
        for k in range(NK):
            items.append((ag_r[k], P, AGW, agbf[:, k * AGW:(k + 1) * AGW]))
        for c in range((D + OS - 1) // OS):
            cw = min(OS, D - c * OS)
            items.append((bfl_d[:, c * OS:c * OS + cw], P, cw, bflbf[:, c * OS:c * OS + cw]))
        for pi, (src, pp, w, dst) in enumerate(items):
            i = pi % 6
            deng = "q" if i < 3 else "s"
            if pi >= 6:
                PR.wait(deng, f"pc_{pi - 6}")
            PR.op(deng, lambda e, sl=pre6[i], src=src, pp=pp, w=w: e.dma_start(
                out=sl[0:pp, 0:w], in_=src), ev=f"pl_{pi}", dma=True, sem=f"p{i}")
            PR.wait("v", f"pl_{pi}")
            PR.op("v", lambda e, sl=pre6[i], pp=pp, w=w, dst=dst: e.tensor_copy(
                out=dst, in_=sl[0:pp, 0:w]), ev=f"pc_{pi}")
        PR.op("v", lambda e: e.memset(ones[:], 1.0))

        emit_block(PR.blocks[-1])

        def mm_phase(ph, w_r):
            sfx = str(ph)

            xsl6 = xstg + x1tmp
            NH = max(1, TOK // OS)
            W = TOK // NH
            first_new = 2 if ph == 2 else 0
            inj = min(10, NJ * NCJ - 1)

            def emit_hwt(e=None):
                PR.wait("q", f"hw_{NT - 1}")
                PR.op("q", lambda e: e.dma_start_transpose(
                    out=hwT[:].rearrange("p (k m) -> p k m", m=P),
                    in_=hwstg[:].rearrange("p (k m) -> p k m", m=P)),
                    ev="trhw", dma=True, sem="p2")

            if ph == 2 and inj < first_new:
                emit_hwt()
            for idx in range(first_new, NJ * NCJ):
                j, c = divmod(idx, NCJ)
                if ph == 1 and j == 0:
                    for hh in range(2 * NH):
                        h = 2 * NH * c + hh
                        k = h // NH
                        kk = h % NH
                        i = h % 6
                        if h >= 6:
                            PR.wait("q", f"pcx_{h - 6}")
                        PR.op("q", lambda e, sl=xsl6[i], k=k, kk=kk: e.dma_start(
                            out=sl[:, 0:W], in_=xt_r[k][:, kk * W:(kk + 1) * W]),
                            ev=f"plx_{h}", dma=True, sem=f"p{i}")
                        PR.wait("v", f"plx_{h}")
                        PR.op("v", lambda e, sl=xsl6[i], k=k, kk=kk: e.tensor_copy(
                            out=xt[:, k * TOK + kk * W: k * TOK + (kk + 1) * W],
                            in_=sl[:, 0:W]), ev=f"pcx_{h}")
                if ph == 2 and idx == inj and inj >= first_new:
                    emit_hwt()
                if idx >= 2:
                    PR.wait("q", f"cw{sfx}_{idx - 2}")
                q_dma(wstg[idx % 2][:, 0:OS], w_r[2 * c][:, j * OS:(j + 1) * OS], idx % 2)
                q_dma(wstg[idx % 2][:, OS:2 * OS], w_r[2 * c + 1][:, j * OS:(j + 1) * OS],
                      idx % 2, ev=f"lw{sfx}_{idx}")

            def s_cast(ci):
                PR.wait("s", f"lw{sfx}_{ci}")
                if ci >= 3:
                    PR.wait("s", f"mmw{sfx}_{ci - 3}")
                PR.op("s", lambda e, b=ci % 3, w=ci % 2: e.copy(
                    out=wbf[b][:], in_=wstg[w][:, 0:1024]), ev=f"cw{sfx}_{ci}")

            def s_x1act(ia):
                j, t = divmod(ia, NT)
                if ia >= 4:
                    PR.wait("s", f"stt{sfx}_{ia - 4}")
                PR.op("s", lambda e, b=ia % 4, t=t, j=j: e.activation(
                    out=x1tmp[b][:], in_=sbf[:, t * D + j * OS: t * D + (j + 1) * OS],
                    func=AF.Identity, bias=negminv[:, t:t + 1], scale=inv[:, t:t + 1]),
                    ev=f"xb_{ia}")

            vsq_tail = []

            def s_sq(idx):
                j, t = divmod(idx, NT)
                col = t * NJ + j
                if j == NJ - 1 and t % 2 == 1:
                    vsq_tail.append(idx)
                    return
                PR.wait("s", f"stt{sfx}_{idx}")
                if idx >= 1 and not (j == NJ - 1 and t == 1):
                    prev = idx - 1 if not (j == NJ - 1 and t >= 2) else idx - 2
                    PR.wait("s", f"sq{sfx}_{prev}")
                PR.op("s", lambda e, t=t, j=j, col=col: e.activation(
                    out=sqscr[:], in_=sbf[:, t * D + j * OS: t * D + (j + 1) * OS],
                    func=AF.Square, accum_out=ssq[:, col:col + 1]), ev=f"sq{sfx}_{idx}")

            half = NT // 2
            if ph == 1:
                for j in range(NJ):
                    if j == 0:
                        for c in range(NCJ):
                            s_cast(c)
                    if j < NJ - 1:
                        for c in range(min(3, NCJ)):
                            s_cast((j + 1) * NCJ + c)
                    for t in range(NT):
                        s_sq(j * NT + t)
                    if j < NJ - 1:
                        for c in range(3, NCJ):
                            s_cast((j + 1) * NCJ + c)
            else:
                for j in range(NJ):
                    if j == 0:
                        for c in range(2, NCJ):
                            s_cast(c)
                        for t in range(half):
                            s_x1act(t)
                    if j < NJ - 1:
                        s_cast((j + 1) * NCJ)
                    for t in range(half, NT):
                        s_x1act(j * NT + t)
                    if j < NJ - 1:
                        s_cast((j + 1) * NCJ + 1)
                        if NCJ > 2:
                            s_cast((j + 1) * NCJ + 2)
                        for t in range(half):
                            s_x1act((j + 1) * NT + t)
                    for t in range(NT):
                        s_sq(j * NT + t)
                    if j < NJ - 1:
                        for c in range(3, NCJ):
                            s_cast((j + 1) * NCJ + c)

            for j in range(NJ):
                boff = j * OS
                for t in range(NT):
                    if j > 0:
                        PR.wait("t", f"stt{sfx}_{(j - 1) * NT + t}")
                    elif ph == 2:
                        PR.wait("t", f"hw_{t}")
                    PR.op("t", lambda e, t=t, boff=boff: e.matmul(
                        ps[t][:], ones[0:1, 0:P], bbf[0:1, boff:boff + OS],
                        start=True, stop=False))
                for c in range(NCJ):
                    ci = j * NCJ + c
                    PR.wait("t", f"cw{sfx}_{ci}")
                    for kk in range(2):
                        k = 2 * c + kk
                        if ph == 1 and j == 0:
                            PR.wait("t", f"pcx_{(k + 1) * max(1, TOK // OS) - 1}")
                        last_k = (c == NCJ - 1 and kk == 1)
                        for t in range(NT):
                            ev = f"mmw{sfx}_{ci}" if (kk == 1 and t == NT - 1) else None
                            if last_k and ph == 1:
                                evy = f"y{sfx}_{j * NT + t}"
                                PR.op("t", lambda e, k=k, t=t, b=ci % 3, kk=kk: e.matmul(
                                    ps[t][:], xt[:, k * TOK + t * P: k * TOK + (t + 1) * P],
                                    wbf[b][:, kk * OS:(kk + 1) * OS],
                                    start=False, stop=True), ev=evy)
                                if t == NT - 1:
                                    PR.ev[f"mmw{sfx}_{ci}"] = PR.ev[evy]
                            else:
                                PR.op("t", lambda e, k=k, t=t, b=ci % 3, kk=kk: e.matmul(
                                    ps[t][:], xt[:, k * TOK + t * P: k * TOK + (t + 1) * P],
                                    wbf[b][:, kk * OS:(kk + 1) * OS],
                                    start=False, stop=False), ev=ev)
                if ph == 2:
                    for t in range(NT):
                        if j == 0 and t == 0:
                            PR.wait("t", "trhw")
                        PR.op("t", lambda e, t=t, boff=boff: e.matmul(
                            ps[t][:], hwT[:, t * P:(t + 1) * P], bflbf[:, boff:boff + OS],
                            start=False, stop=True), ev=f"y{sfx}_{j * NT + t}")

            if ph == 1:
                xslots = xstg + x1tmp
                nsl = len(xslots)
                nh2 = max(1, TOK // OS)
                htot = 2 * nh2 * NCJ
                for idx in range(NS):
                    j, t = divmod(idx, NT)
                    if idx < nsl:
                        hl = ((htot - 1 - idx % nsl) // nsl) * nsl + idx % nsl
                        if hl >= 0:
                            PR.wait("g", f"pcx_{hl}")
                    else:
                        PR.wait("g", f"stt{sfx}_{idx - nsl}")
                    PR.op("g", lambda e, b=idx % nsl, t=t, j=j: e.dma_start(
                        out=xslots[b][:], in_=xn_d[t * P:(t + 1) * P, j * OS:(j + 1) * OS]),
                        ev=f"lx_{idx}", dma=True, sem=f"g{idx % nsl}")

            for idx in range(NS):
                j, t = divmod(idx, NT)
                col = t * NJ + j
                PR.wait("v", f"y{sfx}_{idx}")
                if ph == 1:
                    PR.wait("v", f"lx_{idx}")
                else:
                    PR.wait("v", f"xb_{idx}")
                pool = (xstg + x1tmp) if ph == 1 else x1tmp
                PR.op("v", lambda e, t=t, j=j, b=idx % len(pool), pool=pool,
                      col=col: e.scalar_tensor_tensor(
                          out=sbf[:, t * D + j * OS: t * D + (j + 1) * OS],
                          in0=ps[t][:], scalar=0.0,
                          in1=pool[b][:],
                          op0=ALU.add, op1=ALU.add,
                          accum_out=ssum[:, col:col + 1]), ev=f"stt{sfx}_{idx}")
            for n_, idx in enumerate(vsq_tail):
                j, t = divmod(idx, NT)
                col = t * NJ + j
                PR.wait("v", f"stt{sfx}_{idx}")
                if n_ >= 1:
                    PR.wait("v", f"sq{sfx}_{vsq_tail[n_ - 1]}")
                PR.op("v", lambda e, t=t, j=j, col=col: e.scalar_tensor_tensor(
                    out=xstg[0][:], in0=sbf[:, t * D + j * OS: t * D + (j + 1) * OS],
                    scalar=1.0, in1=sbf[:, t * D + j * OS: t * D + (j + 1) * OS],
                    op0=ALU.mult, op1=ALU.mult,
                    accum_out=ssq[:, col:col + 1]), ev=f"sq{sfx}_{idx}")

        def ln_stats(ph):
            sfx = str(ph)
            PR.op("v", lambda e: e.reduce_sum(
                mean[:], ssum[:].rearrange("p (t j) -> p t j", j=NJ), axis=AX.X),
                ev=f"r1_{sfx}")
            PR.op("v", lambda e: e.reduce_sum(
                msq[:], ssq[:].rearrange("p (t j) -> p t j", j=NJ), axis=AX.X),
                ev=f"r2_{sfx}")
            PR.wait("v", f"r1_{sfx}")
            PR.op("v", lambda e: e.tensor_scalar_mul(out=mean[:], in0=mean[:], scalar1=1.0 / D),
                  ev=f"mn_{sfx}")
            PR.wait("v", f"r2_{sfx}")
            PR.op("v", lambda e: e.tensor_scalar_mul(out=msq[:], in0=msq[:], scalar1=1.0 / D))
            PR.wait("v", f"mn_{sfx}")
            PR.op("v", lambda e: e.tensor_mul(m2t[:], mean[:], mean[:]), ev=f"m2_{sfx}")
            PR.wait("v", f"m2_{sfx}")
            PR.op("v", lambda e: e.tensor_sub(var[:], msq[:], m2t[:]), ev=f"va_{sfx}")
            PR.wait("v", f"va_{sfx}")
            PR.op("v", lambda e: e.tensor_scalar_add(out=veps[:], in0=var[:], scalar1=1e-5),
                  ev=f"ve_{sfx}")
            PR.wait("v", f"ve_{sfx}")
            PR.op("v", lambda e: e.reciprocal(out=rin[:], in_=veps[:]), ev=f"ri_{sfx}")
            PR.wait("s", f"ri_{sfx}")
            PR.op("s", lambda e: e.activation(out=inv0[:], in_=rin[:], func=AF.Sqrt),
                  ev=f"i0_{sfx}")
            PR.wait("s", f"mn_{sfx}")
            PR.op("s", lambda e: e.mul(out=negmean[:], in_=mean[:], mul=-1.0), ev=f"ng_{sfx}")
            PR.wait("v", f"i0_{sfx}")
            PR.op("v", lambda e: e.tensor_scalar(out=invh[:], in0=veps[:], scalar1=0.5,
                                                 scalar2=None, op0=ALU.mult), ev=f"ih_{sfx}")
            PR.op("v", lambda e: e.tensor_mul(nt1[:], inv0[:], inv0[:]), ev=f"n1_{sfx}")
            PR.wait("v", f"n1_{sfx}")
            PR.op("v", lambda e: e.tensor_mul(nt1[:], nt1[:], invh[:]), ev=f"n2_{sfx}")
            PR.wait("v", f"n2_{sfx}")
            PR.op("v", lambda e: e.tensor_scalar(out=nt1[:], in0=nt1[:], scalar1=-1.0,
                                                 scalar2=1.5, op0=ALU.mult, op1=ALU.add),
                  ev=f"n3_{sfx}")
            PR.wait("v", f"n3_{sfx}")
            PR.op("v", lambda e: e.tensor_mul(inv[:], inv0[:], nt1[:]), ev=f"iv_{sfx}")
            PR.wait("v", f"ng_{sfx}")
            PR.wait("v", f"iv_{sfx}")
            PR.op("v", lambda e: e.tensor_mul(negminv[:], negmean[:], inv[:]), ev=f"nmi_{sfx}")

        with ExitStack() as ps_es:
            ps = [ps_es.enter_context(nc.psum_tensor(f"ps{i}", [P, OS], F32))
                  for i in range(NT)]
            PR.new_block()
            mm_phase(1, w1_r)
            emit_block(PR.blocks[-1])

        xsl = [x1s[:, 0:OS], x1s[:, OS:2 * OS], x1s[:, 2 * OS:3 * OS]]
        with ExitStack() as ps_es:
            pst = [ps_es.enter_context(nc.psum_tensor(f"pst{i}", [P, 1024], BF16))
                   for i in range(3)]
            pwm = [ps_es.enter_context(nc.psum_tensor(f"pwm{i}", [P, OS], F32))
                   for i in range(2)]
            PR.new_block()
            for wi in range(24):
                if wi >= 2:
                    PR.wait("t", f"wrm_{wi - 2}")
                PR.op("t", lambda e, b=wi % 2: e.matmul(
                    pwm[b][:], xt[:, 0:P], bflbf[:, 0:OS], start=True, stop=True),
                    ev=f"wrm_{wi}")
            ln_stats(1)
            for idx in range(NT * NKG):
                t, kg = divmod(idx, NKG)
                if idx == 0:
                    PR.wait("s", "nmi_1")
                if idx >= 3:
                    PR.wait("s", f"tr1_{idx - 3}")
                PR.op("s", lambda e, b=idx % 3, t=t, kg=kg: e.activation(
                    out=xsl[b], in_=sbf[:, t * D + kg * OS: t * D + (kg + 1) * OS],
                    func=AF.Identity, bias=negminv[:, t:t + 1], scale=inv[:, t:t + 1]),
                    ev=f"xa_{idx}")
                PR.wait("t", f"xa_{idx}")
                if idx >= 3:
                    PR.wait("t", f"xe_{idx - 3}")
                for q in range(4):
                    PR.op("t", lambda e, b=idx % 3, q=q: e.transpose(
                        pst[b][:, q * P:(q + 1) * P], xsl[b][:, q * P:(q + 1) * P],
                        idbf[:]), ev=f"tr1_{idx}" if q == 3 else None)
                PR.wait("v", f"tr1_{idx}")
                PR.op("v", lambda e, b=idx % 3, t=t, kg=kg: e.tensor_copy(
                    out=xt[:].rearrange("p (k m) -> p k m", m=TOK)[
                        :, 4 * kg:4 * kg + 4, t * P:(t + 1) * P],
                    in_=pst[b][:, 0:4 * P].rearrange("p (k m) -> p k m", m=P)),
                    ev=f"xe_{idx}")
            emit_block(PR.blocks[-1])

        with ExitStack() as ps_es:
            ps = [ps_es.enter_context(nc.psum_tensor(f"psb{i}", [P, OS], F32))
                  for i in range(NT)]
            psh = ps
            PR.new_block()
            for c in range(nbc):
                cw = min(1024, D - c * 1024)
                q_dma(wstg[c % 2][0:1, 0:cw], b2_d[:, c * 1024:c * 1024 + cw], c % 2,
                      ev=f"lb2_{c}", wait_ev=(f"cb2_{c - 2}" if c >= 2 else None))
                v_cast(bbf[0:1, c * 1024:c * 1024 + cw], wstg[c % 2][0:1, 0:cw],
                       ev=f"cb2_{c}", wait_ev=f"lb2_{c}")
            for idx in range(2):
                j, c = divmod(idx, NCJ)
                if nbc > idx:
                    PR.wait("q", f"cb2_{max(0, nbc - 2 + idx)}")
                q_dma(wstg[idx % 2][:, 0:OS], w2_r[2 * c][:, j * OS:(j + 1) * OS], idx % 2)
                q_dma(wstg[idx % 2][:, OS:2 * OS], w2_r[2 * c + 1][:, j * OS:(j + 1) * OS],
                      idx % 2, ev=f"lw2_{idx}")
            for ci in range(2):
                PR.wait("s", f"lw2_{ci}")
                PR.op("s", lambda e, b=ci % 3, w=ci % 2: e.copy(
                    out=wbf[b][:], in_=wstg[w][:, 0:1024]), ev=f"cw2_{ci}")
            for t in range(NT):
                for k in range(NK):
                    PR.op("t", lambda e, t=t, k=k: e.matmul(
                        psh[t][:, 0:AGW], xt[:, k * TOK + t * P: k * TOK + (t + 1) * P],
                        agbf[:, k * AGW:(k + 1) * AGW],
                        start=(k == 0), stop=(k == NK - 1)),
                        ev=f"h_{t}" if k == NK - 1 else None)
            NH2 = 2 if NT >= 2 else 1
            TH = NT // NH2
            for h2 in range(NH2):
                t0, t1 = h2 * TH, (h2 + 1) * TH
                sfh = f"_{h2}"
                for t in range(t0, t1):
                    PR.wait("v", f"h_{t}")
                    PR.op("v", lambda e, t=t: e.tensor_copy(
                        out=lg[:, t * E:(t + 1) * E], in_=psh[t][:, ER:ER + E]),
                        ev=f"lg{sfh}" if t == t1 - 1 else None)
                PR.wait("v", f"lg{sfh}")
                PR.op("v", lambda e, t0=t0, t1=t1: e.reduce_max(
                    m1a[:, t0:t1],
                    lg[:, t0 * E:t1 * E].rearrange("p (t e) -> p t e", e=E), axis=AX.X),
                    ev=f"m1all{sfh}")
                PR.wait("s", f"m1all{sfh}")
                PR.op("s", lambda e, t0=t0, t1=t1: e.mul(
                    out=negm1a[:, t0:t1], in_=m1a[:, t0:t1], mul=-1.0), ev=f"ngall{sfh}")
                PR.wait("s", f"ngall{sfh}")
                for t in range(t0, t1):
                    PR.op("s", lambda e, t=t: e.activation(
                        out=ex64[:, t * E:(t + 1) * E], in_=lg[:, t * E:(t + 1) * E],
                        func=AF.Exp, bias=negm1a[:, t:t + 1]),
                        ev=f"exall{sfh}" if t == t1 - 1 else None)
                PR.wait("v", f"m1all{sfh}")
                for t in range(t0, t1):
                    PR.op("v", lambda e, t=t: e.tensor_single_scalar(
                        out=mask64[:, t * E:(t + 1) * E], in_=lg[:, t * E:(t + 1) * E],
                        scalar=m1a[:, t:t + 1], op=ALU.is_equal),
                        ev=f"mkall{sfh}" if t == t1 - 1 else None)
                PR.wait("v", f"mkall{sfh}")
                for t in range(t0, t1):
                    PR.op("v", lambda e, t=t: e.scalar_tensor_tensor(
                        out=lm64[:, t * E:(t + 1) * E], in0=mask64[:, t * E:(t + 1) * E],
                        scalar=-1e30, in1=lg[:, t * E:(t + 1) * E],
                        op0=ALU.mult, op1=ALU.add),
                        ev=f"lmall{sfh}" if t == t1 - 1 else None)
                PR.wait("v", f"lmall{sfh}")
                PR.op("v", lambda e, t0=t0, t1=t1: e.reduce_max(
                    m2a[:, t0:t1],
                    lm64[:, t0 * E:t1 * E].rearrange("p (t e) -> p t e", e=E), axis=AX.X),
                    ev=f"m2all{sfh}")
                PR.wait("s", f"m2all{sfh}")
                for t in range(t0, t1):
                    PR.op("s", lambda e, t=t: e.activation(
                        out=em21[:, t:t + 1], in_=m2a[:, t:t + 1], func=AF.Exp,
                        bias=negm1a[:, t:t + 1]),
                        ev=f"e21all{sfh}" if t == t1 - 1 else None)
                PR.wait("v", f"e21all{sfh}")
                PR.op("v", lambda e, t0=t0, t1=t1: e.tensor_scalar_add(
                    out=den[:, t0:t1], in0=em21[:, t0:t1], scalar1=1.0), ev=f"dnall{sfh}")
                PR.wait("v", f"dnall{sfh}")
                PR.op("v", lambda e, t0=t0, t1=t1: e.reciprocal(
                    out=dinv[:, t0:t1], in_=den[:, t0:t1]), ev=f"diall{sfh}")
                PR.wait("v", f"m2all{sfh}")
                for t in range(t0, t1):
                    PR.op("v", lambda e, t=t: e.tensor_single_scalar(
                        out=ge64[:, t * E:(t + 1) * E], in_=lg[:, t * E:(t + 1) * E],
                        scalar=m2a[:, t:t + 1], op=ALU.is_ge),
                        ev=f"geall{sfh}" if t == t1 - 1 else None)
                PR.wait("v", f"geall{sfh}")
                PR.wait("v", f"diall{sfh}")
                for t in range(t0, t1):
                    PR.op("v", lambda e, t=t: e.scalar_tensor_tensor(
                        out=wa64[:, t * E:(t + 1) * E], in0=ex64[:, t * E:(t + 1) * E],
                        scalar=dinv[:, t:t + 1], in1=ge64[:, t * E:(t + 1) * E],
                        op0=ALU.mult, op1=ALU.mult),
                        ev=f"waall{sfh}" if t == t1 - 1 else None)
                PR.wait("v", f"waall{sfh}")
                for t in range(t0, t1):
                    for ee in range(E):
                        PR.op("v", lambda e, t=t, ee=ee: e.tensor_single_scalar(
                            out=hwstg[:, t * P + ee * R: t * P + (ee + 1) * R],
                            in_=psh[t][:, ee * R:(ee + 1) * R],
                            scalar=wa64[:, t * E + ee:t * E + ee + 1],
                            op=ALU.mult), ev=f"hw_{t}" if ee == E - 1 else None)
            mm_phase(2, w2_r)
            emit_block(PR.blocks[-1])

        PR.new_block()
        ln_stats(2)
        PR.wait("s", "nmi_2")
        PR.wait("v", "iv_2")
        oslots = x1tmp + xstg
        nos = len(oslots)
        for idx in range(NS):
            t, j = divmod(idx, NJ)
            b = idx % nos
            sl = sbf[:, t * D + j * OS: t * D + (j + 1) * OS]
            aeng = "s" if idx % 2 == 0 else "v"
            if idx >= nos:
                PR.wait(aeng, f"so_{idx - nos}")
            if aeng == "s":
                PR.op("s", lambda e, b=b, sl=sl, t=t: e.activation(
                    out=oslots[b][:], in_=sl, func=AF.Identity,
                    bias=negminv[:, t:t + 1], scale=inv[:, t:t + 1]), ev=f"oa_{idx}")
            else:
                PR.op("v", lambda e, b=b, sl=sl, t=t: e.tensor_scalar(
                    out=oslots[b][:], in0=sl, scalar1=mean[:, t:t + 1],
                    scalar2=inv[:, t:t + 1], op0=ALU.subtract, op1=ALU.mult),
                    ev=f"oa_{idx}")
            qeng = "q" if b < 5 else "g"
            PR.wait(qeng, f"oa_{idx}")
            PR.op(qeng, lambda e, b=b, t=t, j=j: e.dma_start(
                out=out_d[t * P:(t + 1) * P, j * OS:(j + 1) * OS], in_=oslots[b][:]),
                ev=f"so_{idx}", dma=True, sem=f"o{idx % nos}")
        for k in range(min(nos, NS)):
            PR.wait("g", f"so_{NS - 1 - k}")
        emit_block(PR.blocks[-1])

    if _refs is None:
        return build_program(D=D, TOK=TOK, _refs=PR.waited)
    return nc


def kernel(L0_attn_x, L0_attn_weight, L0_attn_bias, L0_ffn_W_base, L0_ffn_b_base,
           L0_ffn_W_gate, L0_ffn_A, L0_ffn_B):
    D = 4096
    TOK = 1024
    n_cores = 8
    x2d = np.ascontiguousarray(L0_attn_x.reshape(n_cores * TOK, D))
    w1t = np.ascontiguousarray(L0_attn_weight.T)
    b1 = np.ascontiguousarray(L0_attn_bias.reshape(1, D))
    w2t = np.ascontiguousarray(L0_ffn_W_base.T)
    b2 = np.ascontiguousarray(L0_ffn_b_base.reshape(1, D))
    ag = np.ascontiguousarray(
        np.concatenate([L0_ffn_A.reshape(ER, D), L0_ffn_W_gate], axis=0).T)
    bfl = np.ascontiguousarray(L0_ffn_B.transpose(0, 2, 1).reshape(ER, D))
    ident = np.eye(P, dtype=np.float32)

    nc = build_program(D=D, TOK=TOK)
    in_maps = []
    for c in range(n_cores):
        xn_c = np.ascontiguousarray(x2d[c * TOK:(c + 1) * TOK])
        xt_c = np.ascontiguousarray(xn_c.T)
        in_maps.append({"xt": xt_c, "xn": xn_c, "w1t": w1t, "b1": b1,
                        "w2t": w2t, "b2": b2, "ag": ag, "bfl": bfl, "ident": ident})
    res = run_bass_kernel_spmd(nc, in_maps, core_ids=list(range(n_cores)))
    out = np.concatenate([np.asarray(r["out"]) for r in res.results], axis=0)
    return out.reshape(L0_attn_x.shape)
